# version 04
# baseline (speedup 1.0000x reference)
import sys
sys.path.insert(0, '/opt/trn_rl_repo')
import numpy as np
import ml_dtypes
import concourse.bass as bass
import concourse.mybir as mybir
from concourse.bass_utils import run_bass_kernel_spmd
from contextlib import ExitStack

F32 = mybir.dt.float32
BF16 = mybir.dt.bfloat16
I32 = mybir.dt.int32
AF = mybir.ActivationFunctionType
ALU = mybir.AluOpType
bfnp = ml_dtypes.bfloat16

CFG = dict(D=4096, I=11008, WTOK=512, WAVES=2, ICH=256, BITERS=13, EPS=1e-5, NCORES=8)


def build_nc(cfg, debug=()):
    D, I, WTOK, WAVES = cfg["D"], cfg["I"], cfg["WTOK"], cfg["WAVES"]
    ICH, BITERS, EPS = cfg["ICH"], cfg["BITERS"], cfg["EPS"]
    NDB = D // 128
    NTIL = WTOK // 128
    NIC = I // ICH
    NIB = I // 128
    IH = I // 2
    NIBH = NIB // 2
    NHC = ICH // 128
    NDC = D // 512
    TOK = WTOK * WAVES

    nc = bass.Bass("TRN2", target_bir_lowering=False)
    es = ExitStack()

    x_in = nc.dram_tensor("x_in", [TOK, D], F32, kind="ExternalInput").ap()
    wg_f = nc.dram_tensor("wg_f", [I, D], F32, kind="ExternalInput").ap()
    wu_f = nc.dram_tensor("wu_f", [I, D], F32, kind="ExternalInput").ap()
    wd_f = nc.dram_tensor("wd_f", [D, I], F32, kind="ExternalInput").ap()
    wq_i = nc.dram_tensor("wq_i", [D, D], I32, kind="ExternalInput").ap()
    sc_in = nc.dram_tensor("sc_in", [128, NDB], F32, kind="ExternalInput").ap()
    bi_in = nc.dram_tensor("bi_in", [128, NDB], F32, kind="ExternalInput").ap()
    ones_in = nc.dram_tensor("ones_in", [1, 128], BF16, kind="ExternalInput").ap()
    onec_in = nc.dram_tensor("onec_in", [128, 1], BF16, kind="ExternalInput").ap()
    idb_in = nc.dram_tensor("idb_in", [128, 128], BF16, kind="ExternalInput").ap()
    idf_in = nc.dram_tensor("idf_in", [128, 128], F32, kind="ExternalInput").ap()

    out_t = nc.dram_tensor("out_t", [TOK, D], F32, kind="ExternalOutput").ap()

    wg_b = nc.dram_tensor("wg_b", [I, D], BF16, kind="Internal").ap()
    wu_b = nc.dram_tensor("wu_b", [I, D], BF16, kind="Internal").ap()
    wd_b = nc.dram_tensor("wd_b", [D, I], BF16, kind="Internal").ap()
    wq_b = nc.dram_tensor("wq_b", [D, D], BF16, kind="Internal").ap()
    xsp = nc.dram_tensor("xsp", [WAVES, NDB, 128, WTOK], F32, kind="Internal").ap()
    x2s = nc.dram_tensor("x2s", [WAVES, NDB, 128, WTOK], F32, kind="Internal").ap()

    taps = {}
    def mktap(name, shape, dtype):
        if name in debug:
            taps[name] = nc.dram_tensor("tap_" + name, shape, dtype, kind="ExternalOutput").ap()
    mktap("s0", [128, NIB * 128], BF16)
    mktap("mt", [128, NTIL, NIB, 128], BF16)
    mktap("tfin", [128, NTIL], F32)
    mktap("hi", [128, NTIL], F32)
    mktap("xn1", [128, NDB, WTOK], BF16)
    mktap("xn2", [128, NDB, WTOK], BF16)
    mktap("x1", [128, NDB, WTOK], F32)
    mktap("x2", [128, NDB, WTOK], F32)

    POOLB = NTIL * NIB * 128 * 2
    XTB = NDB * WTOK * 4
    assert XTB <= POOLB
    XNB = NDB * WTOK * 2
    WSB = max(2 * 2 * NDB * 128 * 2,
              2 * NDB * 128 * 2,
              3 * NIBH * 128 * 2)
    TRB = 2 * IH * 2
    off = 0
    def nxt(n, align=64):
        nonlocal off
        o = (off + align - 1) // align * align
        off = o + n
        return o
    O_POOL = nxt(POOLB)
    O_XN = nxt(XNB)
    O_WS = nxt(WSB)
    O_TR = nxt(TRB)
    XSW = max(512, WTOK)
    O_XS = nxt(2 * XSW * 4)
    O_FS = nxt(2 * WTOK * 4)
    O_X2 = nxt(2 * WTOK * 4)
    O_OS = nxt(2 * WTOK * 4)
    O_MA = nxt(4 * ICH * 2)
    O_MB = nxt(4 * ICH * 4)
    O_SQ = nxt(2 * WTOK * 2)
    O_NS = nxt(2 * WTOK * 2)
    O_MISC = nxt(10304)
    TOTAL = off
    backing = es.enter_context(nc.sbuf_tensor("arena", [128, TOTAL], mybir.dt.uint8))
    base_addr = nc.lookup_mloc(backing).addr

    def at(name, shape, dtype, offset):
        h = nc.virtual_sbuf_tensor(name, shape, dtype)
        nc._tensor(h.name, shape, dtype, type="SB")
        mloc = nc.lookup_mloc(h)
        mloc.allocated = True
        mloc.addr = base_addr + offset
        return h

    pool = at("pool", [128, NTIL, NIB, 128], BF16, O_POOL)
    pool2 = at("pool2", [128, NTIL, NIB * 128], BF16, O_POOL)
    xT = at("xT", [128, NDB, WTOK], F32, O_POOL)
    xn = at("xn", [128, NDB, WTOK], BF16, O_XN)
    wsg = at("wsg", [128, 2, 2, NDB, 128], BF16, O_WS)
    wsq = at("wsq", [128, 2, NDB, 128], BF16, O_WS)
    wsd = at("wsd", [128, 3, NIBH, 128], BF16, O_WS)
    trv = at("trv", [128, IH], BF16, O_TR)
    trs = at("trs", [128, IH], BF16, O_TR + IH * 2)
    xst = at("xst", [128, 2, XSW], F32, O_XS)
    fst = at("fst", [128, 2, WTOK], F32, O_FS)
    x2t = at("x2t", [128, 2, WTOK], F32, O_X2)
    ost = at("ost", [128, 2, WTOK], F32, O_OS)
    mska = at("mska", [128, 4, ICH], BF16, O_MA)
    mskb = at("mskb", [128, 4, ICH], F32, O_MB)
    sqst = at("sqst", [128, 2, WTOK], BF16, O_SQ)
    nst = at("nst", [128, 2, WTOK], BF16, O_NS)
    mo = O_MISC
    def misc(name, shape, dtype):
        nonlocal mo
        nbytes = int(np.prod(shape[1:])) * mybir.dt.size(dtype)
        t = at(name, shape, dtype, mo)
        mo += (nbytes + 63) // 64 * 64
        return t
    onesb = misc("onesb", [1, 128], BF16)
    onec = misc("onec", [128, 1], BF16)
    idb = misc("idb", [128, 128], BF16)
    idf = misc("idf", [128, 128], F32)
    schalf = misc("schalf", [128, NDB], F32)
    bihalf = misc("bihalf", [128, NDB], F32)
    hi4 = misc("hi4", [128, NTIL], F32)
    t4 = misc("t4", [128, 2, NTIL], F32)
    st4 = misc("st4", [128, 2, NTIL], F32)
    tps4 = misc("tps4", [128, NTIL], F32)
    negt4 = misc("negt4", [128, NTIL], F32)
    cntp = misc("cntp", [128, 2 * NTIL], F32)
    sgns = misc("sgns", [128, 2 * NTIL], F32)
    cp4 = misc("cp4", [128, NTIL], F32)
    sg4 = misc("sg4", [128, NTIL], F32)
    h14 = misc("h14", [128, NTIL], F32)
    gs4 = misc("gs4", [128, NTIL], F32)
    tf4 = misc("tf4", [128, NTIL], F32)
    mu_f = misc("mu_f", [1, WTOK], F32)
    v_a = misc("v_a", [1, WTOK], F32)
    v_b = misc("v_b", [1, WTOK], F32)
    mu_bf = misc("mu_bf", [1, WTOK], BF16)
    rs_bf = misc("rs_bf", [1, WTOK], BF16)
    assert mo <= O_MISC + 10304, f"misc overflow {mo - O_MISC}"
    try:
        rem = nc.sbuf_bytes_remaining()
    except Exception:
        rem = None
    print(f"[build] arena {TOTAL} B/partition; sbuf remaining {rem}")

    PB = [es.enter_context(nc.psum_tensor(f"pb{i}", [128, 512], F32)) for i in range(8)]

    semnames = ["cast", "xin", "tin", "txp", "vxc", "sxb", "vsq", "tst", "vln", "sln",
                "vln2", "tbc", "vn", "wqy", "wqs", "tff", "ssf", "vvf", "spl", "gy",
                "gs_", "tg", "sev", "vtp", "sbi", "vbi", "uy", "us_", "tu", "vm",
                "tmt", "smt", "dy", "ds_", "td", "xar", "vd", "x2w", "sx3", "x2r",
                "out", "vres",
                "cwq", "cwg", "cwu", "cwd",
                "xld0", "xld1", "xld2",
                "xar0", "xar1", "xar2",
                "x2w0", "x2w1",
                "outa", "outb"]
    SEM, CNT = {}, {}
    for s in semnames:
        SEM[s] = es.enter_context(nc.semaphore("sem_" + s))
        CNT[s] = 0
    def inc(instr, name, k):
        instr.then_inc(SEM[name], k)
        CNT[name] += k
        return CNT[name]
    def W(eng, name, val=None):
        eng.wait_ge(SEM[name], CNT[name] if val is None else val)
    def nops(v, n=3):
        v.drain()

    block = es.enter_context(nc.Block())
    cast_marks = {}

    @block.gpsimd
    def _(g):
        inc(g.dma_start(out=onesb[:], in_=ones_in[:, :]), "xin", 16)
        inc(g.dma_start(out=onec[:], in_=onec_in[:, :]), "xin", 16)
        inc(g.dma_start(out=idb[:], in_=idb_in[:, :]), "xin", 16)
        inc(g.dma_start(out=idf[:], in_=idf_in[:, :]), "xin", 16)
        inc(g.dma_start(out=schalf[:], in_=sc_in[:, :]), "xin", 16)
        inc(g.dma_start(out=bihalf[:], in_=bi_in[:, :]), "xin", 16)
        nch = max(1, D // 1024)
        for j in range(nch):
            r0, r1 = j * D // nch, (j + 1) * D // nch
            inc(g.dma_start(out=wq_b[r0:r1, :], in_=wq_i[r0:r1, :]), "cwq", 16)
        cast_marks["wq"] = ("cwq", CNT["cwq"])
        for j in range(8):
            r0, r1 = j * I // 8, (j + 1) * I // 8
            inc(g.dma_start(out=wg_b[r0:r1, :], in_=wg_f[r0:r1, :]), "cwg", 16)
        cast_marks["wg"] = ("cwg", CNT["cwg"])
        for j in range(8):
            r0, r1 = j * I // 8, (j + 1) * I // 8
            inc(g.dma_start(out=wu_b[r0:r1, :], in_=wu_f[r0:r1, :]), "cwu", 16)
        cast_marks["wu"] = ("cwu", CNT["cwu"])
        for j in range(8):
            r0, r1 = j * D // 8, (j + 1) * D // 8
            inc(g.dma_start(out=wd_b[r0:r1, :], in_=wd_f[r0:r1, :]), "cwd", 16)
        cast_marks["wd"] = ("cwd", CNT["cwd"])

    @block.vector
    def _(v):
        W(v, "xin")
        v.tensor_scalar(schalf[:], schalf[:], 0.5, None, ALU.mult)
        nops(v)
        ii = v.tensor_scalar(bihalf[:], bihalf[:], 0.5, None, ALU.mult)
        inc(ii, "vres", 1)
        nops(v)

    def ln_stats_pass(wait_name, wait_val):
        @block.scalar
        def _(s):
            s.wait_ge(SEM[wait_name], wait_val)
            for db in range(NDB):
                ii = s.activation(xn[:, db, :], xT[:, db, :], AF.Identity)
                inc(ii, "sxb", 1)
        base_sxb = CNT["sxb"] - NDB
        base_tst = CNT["tst"]
        @block.vector
        def _(v):
            for db in range(NDB):
                W(v, "sxb", base_sxb + db + 1)
                if db >= 2:
                    W(v, "tst", base_tst + db - 1)
                ii = v.tensor_tensor(sqst[:, db % 2, :], xn[:, db, :], xn[:, db, :], ALU.mult)
                inc(ii, "vsq", 1)
        base_vsq = CNT["vsq"] - NDB
        @block.tensor
        def _(t):
            for db in range(NDB):
                W(t, "vsq", base_vsq + db + 1)
                t.matmul(PB[2][0:1, 0:WTOK], onec[:, :], xn[:, db, :],
                         start=(db == 0), stop=(db == NDB - 1))
                ii = t.matmul(PB[3][0:1, 0:WTOK], onec[:, :], sqst[:, db % 2, :],
                              start=(db == 0), stop=(db == NDB - 1))
                inc(ii, "tst", 1)

    def ln_finalize_normalize():
        @block.vector
        def _(v):
            W(v, "tst")
            v.tensor_scalar(mu_f[:], PB[2][0:1, 0:WTOK], 1.0 / D, None, ALU.mult)
            v.tensor_scalar(v_a[:], PB[3][0:1, 0:WTOK], 1.0 / D, None, ALU.mult)
            nops(v)
            v.tensor_mul(v_b[:], mu_f[:], mu_f[:])
            nops(v)
            v.tensor_sub(v_a[:], v_a[:], v_b[:])
            nops(v)
            ii = v.tensor_scalar(v_b[:], v_a[:], EPS, None, ALU.add)
            inc(ii, "vln", 1)
        @block.scalar
        def _(s):
            W(s, "vln")
            ii = s.activation(v_a[:], v_b[:], AF.Sqrt)
            inc(ii, "sln", 1)
        @block.vector
        def _(v):
            W(v, "sln")
            v.reciprocal(v_b[:], v_a[:])
            nops(v)
            v.tensor_copy(mu_bf[:], mu_f[:])
            ii = v.tensor_copy(rs_bf[:], v_b[:])
            inc(ii, "vln2", 1)
            nops(v)
        @block.tensor
        def _(t):
            W(t, "vln2")
            t.matmul(PB[4][:, 0:WTOK], onesb[:, :], mu_bf[:, :], start=True, stop=True)
            ii = t.matmul(PB[5][:, 0:WTOK], onesb[:, :], rs_bf[:, :], start=True, stop=True)
            inc(ii, "tbc", 1)
        @block.vector
        def _(v):
            W(v, "tbc")
            ii = None
            for db in range(NDB):
                v.tensor_tensor(nst[:, db % 2, :], xT[:, db, :], PB[4][:, 0:WTOK], ALU.subtract)
                v.drain()
                ii = v.tensor_tensor(xn[:, db, :], nst[:, db % 2, :], PB[5][:, 0:WTOK], ALU.mult)
            inc(ii, "vn", 1)

    def ffn(w, second):
        base_tff = CNT["tff"]
        base_ssf = CNT["ssf"]
        base_vvf = CNT["vvf"]
        base_wqy = CNT["wqy"]
        base_wqs = CNT["wqs"]
        base_txp = CNT["txp"]
        base_vxc = CNT["vxc"]
        OUTS = ["outa", "outb"]
        base_outs = [CNT[nm] for nm in OUTS]
        ny = (NDB + 1) // 2
        @block.sync
        def _(s):
            W(s, *cast_marks["wq"])
            W(s, "td")
            W(s, "tff", base_tff)
            for k in range(ny):
                ob = 2 * k
                if k >= 1:
                    W(s, "tff", base_tff + ob - 1)
                ii = s.dma_start_transpose(wsq[:, ob % 2, :, :],
                                           wq_b[ob * 128:(ob + 1) * 128, :])
                inc(ii, "wqy", 16)
        @block.scalar
        def _(s):
            W(s, *cast_marks["wq"])
            W(s, "td")
            W(s, "tff", base_tff)
            W(s, "vres", 1)
            issued = 0
            odd_obs = list(range(1, NDB, 2))
            ii = s.dma_start_transpose(wsq[:, 1, :, :], wq_b[128:256, :])
            inc(ii, "wqs", 16)
            issued = 1
            for ob in range(NDB):
                if ob % 2 == 1 and issued < len(odd_obs) and ob + 2 < NDB:
                    W(s, "tff", base_tff + ob + 1)
                    ii = s.dma_start_transpose(wsq[:, (ob + 2) % 2, :, :],
                                               wq_b[(ob + 2) * 128:(ob + 3) * 128, :])
                    inc(ii, "wqs", 16)
                    issued += 1
                W(s, "tff", base_tff + ob + 1)
                if ob >= 2:
                    W(s, "vvf", base_vvf + ob - 1)
                ii = s.activation(fst[:, ob % 2, :], PB[ob % 2][:, 0:WTOK], AF.Identity,
                                  bias=bihalf[:, ob:ob + 1], scale=schalf[:, ob:ob + 1])
                inc(ii, "ssf", 1)
        def ffn_mm(t, ob):
            if ob % 2 == 0:
                W(t, "wqy", base_wqy + 16 * (ob // 2 + 1))
            else:
                W(t, "wqs", base_wqs + 16 * (ob // 2 + 1))
            if ob >= 2:
                W(t, "ssf", base_ssf + ob - 1)
            ii = None
            for db in range(NDB):
                ii = t.matmul(PB[ob % 2][:, 0:WTOK], wsq[:, ob % 2, db, :],
                              xn[:, db, :], start=(db == 0), stop=(db == NDB - 1))
            inc(ii, "tff", 1)
        def ffn_tp(t, k):
            W(t, "vvf", base_vvf + k + 1)
            if k >= 2:
                W(t, "vxc", base_vxc + k - 1)
            ii = None
            for q in range(NTIL):
                ii = t.transpose(PB[6 + k % 2][:, q * 128:(q + 1) * 128],
                                 ost[:, k % 2, q * 128:(q + 1) * 128], idf[:, :])
            inc(ii, "txp", 1)
        if not second:
            @block.tensor
            def _(t):
                W(t, "vn")
                for ob in range(NDB):
                    ffn_mm(t, ob)
        else:
            @block.tensor
            def _(t):
                W(t, "vn")
                for ob in range(NDB):
                    ffn_mm(t, ob)
                    if ob >= 2:
                        ffn_tp(t, ob - 2)
                ffn_tp(t, NDB - 2)
                ffn_tp(t, NDB - 1)
        if not second:
            @block.vector
            def _(v):
                for ob in range(NDB):
                    W(v, "ssf", base_ssf + ob + 1)
                    ii = v.tensor_tensor(xT[:, ob, :], xT[:, ob, :], fst[:, ob % 2, :], ALU.add)
                    inc(ii, "vvf", 1)
        else:
            def ffn_cp(v, k):
                W(v, "txp", base_txp + k + 1)
                if k >= 2:
                    W(v, OUTS[k % 2], base_outs[k % 2] + 16 * NTIL * (k // 2))
                ii = None
                for q in range(NTIL):
                    ii = v.tensor_copy(x2t[:, k % 2, q * 128:(q + 1) * 128],
                                       PB[6 + k % 2][:, q * 128:(q + 1) * 128])
                inc(ii, "vxc", 1)
            @block.vector
            def _(v):
                for ob in range(NDB):
                    W(v, "ssf", base_ssf + ob + 1)
                    if ob >= 2:
                        W(v, "txp", base_txp + ob - 1)
                    ii = v.tensor_tensor(ost[:, ob % 2, :], xT[:, ob, :], fst[:, ob % 2, :], ALU.add)
                    inc(ii, "vvf", 1)
                    if ob >= 1:
                        ffn_cp(v, ob - 1)
                ffn_cp(v, NDB - 1)
            @block.gpsimd
            def _(g):
                for ob in range(NDB):
                    W(g, "vxc", base_vxc + ob + 1)
                    for q in range(NTIL):
                        ii = g.dma_start(
                            out=out_t[w * WTOK + q * 128:w * WTOK + (q + 1) * 128,
                                      ob * 128:(ob + 1) * 128],
                            in_=x2t[:, ob % 2, q * 128:(q + 1) * 128])
                        inc(ii, OUTS[ob % 2], 16)

    def dump_tap(name, src_ap):
        if name in taps:
            @block.sync
            def _(s):
                for nm in ["vn", "sev", "smt", "vbi", "x2r", "vxc", "vvf"]:
                    W(s, nm)
                ii = s.dma_start(out=taps[name][:, :, :], in_=src_ap)
                inc(ii, "out", 16)

    for w in range(WAVES):
        XLD = ["xld0", "xld1", "xld2"]
        base_xld = [CNT[nm] for nm in XLD]
        base_tin = CNT["tin"]
        base_vxc = CNT["vxc"]
        nidx = NTIL * NDC
        @block.gpsimd
        def _(g):
            W(g, "vd")
            idx = 0
            for T in range(NTIL):
                for dc in range(NDC):
                    if idx >= 2:
                        W(g, "tin", base_tin + idx - 1)
                    ii = g.dma_start(out=xst[:, idx % 2, 0:512],
                                     in_=x_in[w * WTOK + T * 128:w * WTOK + (T + 1) * 128,
                                              dc * 512:(dc + 1) * 512])
                    inc(ii, XLD[idx % 2], 16)
                    idx += 1
        @block.tensor
        def _(t):
            W(t, "xin")
            W(t, "vxc")
            W(t, "td")
            idx = 0
            for T in range(NTIL):
                for dc in range(NDC):
                    W(t, XLD[idx % 2], base_xld[idx % 2] + 16 * (idx // 2 + 1))
                    if idx >= 2:
                        W(t, "vxc", base_vxc + idx - 1)
                    ii = None
                    for b in range(4):
                        ii = t.transpose(PB[6 + idx % 2][:, b * 128:(b + 1) * 128],
                                         xst[:, idx % 2, b * 128:(b + 1) * 128], idf[:, :])
                    inc(ii, "tin", 1)
                    idx += 1
        @block.vector
        def _(v):
            idx = 0
            for T in range(NTIL):
                for dc in range(NDC):
                    W(v, "tin", base_tin + idx + 1)
                    ii = None
                    for b in range(4):
                        db = dc * 4 + b
                        ii = v.tensor_copy(xT[:, db, T * 128:(T + 1) * 128],
                                           PB[6 + idx % 2][:, b * 128:(b + 1) * 128])
                    inc(ii, "vxc", 1)
                    idx += 1
        ln_stats_pass("vxc", CNT["vxc"])
        ln_finalize_normalize()
        ffn(w, second=False)
        ln_stats_pass("vvf", CNT["vvf"])
        ln_finalize_normalize()
        if w == 0:
            dump_tap("xn1", xn[:, :, :]) if "xn1" in taps else None
            dump_tap("x1", xT[:, :, :]) if "x1" in taps else None
        base_spl = CNT["spl"]
        @block.gpsimd
        def _(g):
            W(g, "vn")
            for db in range(NDB):
                ii = g.dma_start(out=xsp[w, db, :, :], in_=xT[:, db, :])
                inc(ii, "spl", 16)
        base_gy = CNT["gy"]; base_gs = CNT["gs_"]
        base_tg = CNT["tg"]; base_sev = CNT["sev"]
        @block.sync
        def _(s):
            W(s, *cast_marks["wg"])
            W(s, "tff")
            for c in range(0, NIC, 2):
                if c >= 2:
                    W(s, "tg", base_tg + 4 * (c - 1))
                for hh in range(2):
                    ii = s.dma_start_transpose(
                        wsg[:, c % 2, hh, :, :],
                        wg_b[c * ICH + hh * 128:c * ICH + (hh + 1) * 128, :])
                    inc(ii, "gy", 16)
        @block.scalar
        def _(s):
            W(s, *cast_marks["wg"])
            W(s, "tff")
            for hh in range(2):
                ii = s.dma_start_transpose(wsg[:, 1, hh, :, :],
                                           wg_b[ICH + hh * 128:ICH + (hh + 1) * 128, :])
                inc(ii, "gs_", 16)
            W(s, "spl", base_spl + 16 * NDB)
            for c in range(NIC):
                if (c + 2) < NIC and (c + 2) % 2 == 1:
                    W(s, "tg", base_tg + 4 * (c + 1))
                    for hh in range(2):
                        ii = s.dma_start_transpose(
                            wsg[:, (c + 2) % 2, hh, :, :],
                            wg_b[(c + 2) * ICH + hh * 128:(c + 2) * ICH + (hh + 1) * 128, :])
                        inc(ii, "gs_", 16)
                for T in range(NTIL):
                    W(s, "tg", base_tg + 4 * c + 4)
                    bank = PB[2 * (c % 2) + T // 2]
                    sl = bank[:, (T % 2) * ICH:(T % 2 + 1) * ICH]
                    ii = s.activation(pool2[:, T, c * ICH:(c + 1) * ICH], sl, AF.Silu)
                    inc(ii, "sev", 1)
        @block.tensor
        def _(t):
            W(t, "vn")
            for c in range(NIC):
                if c % 2 == 0:
                    W(t, "gy", base_gy + 32 * (c // 2 + 1))
                else:
                    W(t, "gs_", base_gs + 32 * (c // 2 + 1))
                if c >= 2:
                    W(t, "sev", base_sev + 4 * (c - 1))
                for T in range(NTIL):
                    bank = PB[2 * (c % 2) + T // 2]
                    sl = bank[:, (T % 2) * ICH:(T % 2 + 1) * ICH]
                    ii = None
                    for db in range(NDB):
                        ii = t.matmul(sl, xn[:, db, T * 128:(T + 1) * 128],
                                      wsg[:, c % 2, :, db, :],
                                      start=(db == 0), stop=(db == NDB - 1))
                    inc(ii, "tg", 1)
        if w == 0 and "s0" in taps:
            @block.sync
            def _(s):
                W(s, "sev")
                ii = s.dma_start(out=taps["s0"][:, :], in_=pool2[:, 0, :])
                inc(ii, "out", 16)
        @block.vector
        def _(v):
            W(v, "sev")
            for T in range(NTIL):
                v.reduce_max(hi4[:, T:T + 1], pool2[:, T, :], mybir.AxisListType.X,
                             apply_absolute_value=True)
            nops(v)
            v.memset(t4[:, 0, :], 0.0)
            v.tensor_scalar(st4[:, 0, :], hi4[:, :], 0.5, None, ALU.mult)
            nops(v)
        for it in range(BITERS):
            pp, qq = it % 2, (it + 1) % 2
            @block.vector
            def _(v):
                ii = v.tensor_tensor(tps4[:, :], t4[:, pp, :], st4[:, pp, :], ALU.add)
                inc(ii, "vtp", 1)
                nops(v)
                for T in range(NTIL):
                    for hh in range(2):
                        if (T, hh) != (0, 0):
                            v.drain()
                        v.tensor_scalar(trv[:, :], pool2[:, T, hh * IH:(hh + 1) * IH],
                                        tps4[:, T:T + 1], None, ALU.is_gt, ALU.add,
                                        accum_out=cntp[:, 2 * T + hh:2 * T + hh + 1])
            @block.scalar
            def _(s):
                W(s, "vtp")
                ii = None
                for T in range(NTIL):
                    for hh in range(2):
                        s.drain()
                        ii = s.activation(trs[:, :], pool2[:, T, hh * IH:(hh + 1) * IH],
                                          AF.Sign, bias=tps4[:, T:T + 1],
                                          accum_out=sgns[:, 2 * T + hh:2 * T + hh + 1])
                inc(ii, "sbi", 1)
            @block.vector
            def _(v):
                W(v, "sbi")
                v.drain()
                v.tensor_tensor(cp4[:, :], cntp[:, 0:2 * NTIL:2], cntp[:, 1:2 * NTIL:2], ALU.add)
                v.tensor_tensor(sg4[:, :], sgns[:, 0:2 * NTIL:2], sgns[:, 1:2 * NTIL:2], ALU.add)
                nops(v)
                v.scalar_tensor_tensor(h14[:, :], sg4[:, :], -0.5, cp4[:, :], ALU.mult, ALU.add)
                nops(v)
                v.scalar_tensor_tensor(gs4[:, :], h14[:, :], 0.0, st4[:, pp, :], ALU.is_gt, ALU.mult)
                nops(v)
                v.tensor_tensor(t4[:, qq, :], t4[:, pp, :], gs4[:, :], ALU.add)
                v.tensor_scalar(st4[:, qq, :], st4[:, pp, :], 0.5, None, ALU.mult)
                nops(v)
        TFIN = BITERS % 2
        @block.vector
        def _(v):
            v.scalar_tensor_tensor(tf4[:, :], st4[:, TFIN, :], 2.0, t4[:, TFIN, :],
                                   ALU.mult, ALU.add)
            nops(v)
            ii = v.tensor_scalar(negt4[:, :], tf4[:, :], -1.0, None, ALU.mult)
            inc(ii, "vbi", 1)
            nops(v)
        bis_done = CNT["vbi"]
        if w == 0 and "tfin" in taps:
            @block.sync
            def _(s):
                W(s, "vbi")
                ii = s.dma_start(out=taps["tfin"][:, :], in_=t4[:, TFIN, :])
                inc(ii, "out", 16)
        if w == 0 and "hi" in taps:
            @block.sync
            def _(s):
                W(s, "vbi")
                ii = s.dma_start(out=taps["hi"][:, :], in_=hi4[:, :])
                inc(ii, "out", 16)
        base_uy = CNT["uy"]; base_us = CNT["us_"]
        base_tu = CNT["tu"]; base_vm = CNT["vm"]
        base_tmt = CNT["tmt"]; base_smt = CNT["smt"]
        @block.sync
        def _(s):
            W(s, *cast_marks["wu"])
            W(s, "tg")
            for c in range(0, NIC, 2):
                if c >= 2:
                    W(s, "tu", base_tu + 4 * (c - 1))
                for hh in range(2):
                    ii = s.dma_start_transpose(
                        wsg[:, c % 2, hh, :, :],
                        wu_b[c * ICH + hh * 128:c * ICH + (hh + 1) * 128, :])
                    inc(ii, "uy", 16)
        @block.scalar
        def _(s):
            W(s, *cast_marks["wu"])
            W(s, "tg")
            for hh in range(2):
                ii = s.dma_start_transpose(wsg[:, 1, hh, :, :],
                                           wu_b[ICH + hh * 128:ICH + (hh + 1) * 128, :])
                inc(ii, "us_", 16)
            for c in range(NIC):
                if (c + 2) < NIC and (c + 2) % 2 == 1:
                    W(s, "tu", base_tu + 4 * (c + 1))
                    for hh in range(2):
                        ii = s.dma_start_transpose(
                            wsg[:, (c + 2) % 2, hh, :, :],
                            wu_b[(c + 2) * ICH + hh * 128:(c + 2) * ICH + (hh + 1) * 128, :])
                        inc(ii, "us_", 16)
                for T in range(NTIL):
                    W(s, "tmt", base_tmt + c * NTIL + T + 1)
                    q = (c * NTIL + T) % 2
                    ii = None
                    for hh in range(NHC):
                        ii = s.activation(pool[:, T, c * NHC + hh, :],
                                          PB[6 + q][:, hh * 128:(hh + 1) * 128], AF.Identity)
                    inc(ii, "smt", 1)
        @block.tensor
        def _(t):
            W(t, "vbi", bis_done)
            for c in range(NIC):
                if c % 2 == 0:
                    W(t, "uy", base_uy + 32 * (c // 2 + 1))
                else:
                    W(t, "us_", base_us + 32 * (c // 2 + 1))
                if c >= 2:
                    W(t, "vm", base_vm + 4 * (c - 1))
                for T in range(NTIL):
                    bank = PB[2 * (c % 2) + T // 2]
                    sl = bank[:, (T % 2) * ICH:(T % 2 + 1) * ICH]
                    ii = None
                    for db in range(NDB):
                        ii = t.matmul(sl, xn[:, db, T * 128:(T + 1) * 128],
                                      wsg[:, c % 2, :, db, :],
                                      start=(db == 0), stop=(db == NDB - 1))
                    inc(ii, "tu", 1)
                for T in range(NTIL):
                    W(t, "vm", base_vm + 4 * c + T + 1)
                    kk = c * NTIL + T
                    if kk >= 2:
                        W(t, "smt", base_smt + kk - 1)
                    q = kk % 2
                    ii = None
                    for hh in range(NHC):
                        ii = t.transpose(PB[6 + q][:, hh * 128:(hh + 1) * 128],
                                         mskb[:, T, hh * 128:(hh + 1) * 128], idf[:, :])
                    inc(ii, "tmt", 1)
        @block.vector
        def _(v):
            for c in range(NIC):
                for T in range(NTIL):
                    W(v, "tu", base_tu + 4 * c + 4)
                    kk = c * NTIL + T
                    if kk >= 4:
                        W(v, "tmt", base_tmt + kk - 3)
                    bank = PB[2 * (c % 2) + T // 2]
                    sl = bank[:, (T % 2) * ICH:(T % 2 + 1) * ICH]
                    spool = pool2[:, T, c * ICH:(c + 1) * ICH]
                    v.scalar_tensor_tensor(mska[:, T, :], spool, tf4[:, T:T + 1],
                                           sl, ALU.is_gt, ALU.mult)
                    v.scalar_tensor_tensor(mskb[:, T, :], spool, negt4[:, T:T + 1],
                                           sl, ALU.is_lt, ALU.mult)
                v.drain()
                for T in range(NTIL):
                    v.tensor_tensor(mska[:, T, :], mska[:, T, :], mskb[:, T, :], ALU.add)
                v.drain()
                for T in range(NTIL):
                    spool = pool2[:, T, c * ICH:(c + 1) * ICH]
                    ii = v.tensor_tensor(mskb[:, T, :], mska[:, T, :], spool, ALU.mult)
                    inc(ii, "vm", 1)
        if w == 0 and "mt" in taps:
            @block.sync
            def _(s):
                W(s, "smt")
                ii = s.dma_start(out=taps["mt"][:, :, :, :], in_=pool[:, :, :, :])
                inc(ii, "out", 16)
        base_dy = CNT["dy"]; base_ds = CNT["ds_"]
        base_td = CNT["td"]; base_vd = CNT["vd"]
        base_xar = CNT["xar"]; base_x2w = CNT["x2w"]; base_sx3 = CNT["sx3"]
        @block.sync
        def _(s):
            W(s, *cast_marks["wd"])
            W(s, "tu")
            for hb in range(0, NDB, 2):
                for hh in range(2):
                    idx = hb * 2 + hh
                    if idx >= 3:
                        W(s, "td", base_td + idx - 2)
                    ii = s.dma_start_transpose(
                        wsd[:, idx % 3, :, :],
                        wd_b[hb * 128:(hb + 1) * 128, hh * IH:(hh + 1) * IH])
                    inc(ii, "dy", 16)
        @block.scalar
        def _(s):
            W(s, *cast_marks["wd"])
            W(s, "tu")
            for hh in range(2):
                idx = 2 + hh
                if idx >= 3:
                    W(s, "td", base_td + idx - 2)
                ii = s.dma_start_transpose(wsd[:, idx % 3, :, :],
                                           wd_b[128:256, hh * IH:(hh + 1) * IH])
                inc(ii, "ds_", 16)
            for hb in range(NDB):
                if hb >= 1:
                    W(s, "vd", base_vd + hb)
                    ii = s.activation(xn[:, hb - 1, :], x2t[:, (hb - 1) % 2, :], AF.Identity)
                    inc(ii, "sx3", 1)
                nhb2 = hb + 2
                if nhb2 < NDB and nhb2 % 2 == 1:
                    for hh in range(2):
                        idx = nhb2 * 2 + hh
                        W(s, "td", base_td + max(0, idx - 2))
                        ii = s.dma_start_transpose(
                            wsd[:, idx % 3, :, :],
                            wd_b[nhb2 * 128:(nhb2 + 1) * 128, hh * IH:(hh + 1) * IH])
                        inc(ii, "ds_", 16)
            W(s, "vd", base_vd + NDB)
            ii = s.activation(xn[:, NDB - 1, :], x2t[:, (NDB - 1) % 2, :], AF.Identity)
            inc(ii, "sx3", 1)
        XAR = ["xar0", "xar1", "xar2"]
        base_xar3 = [CNT[nm] for nm in XAR]
        X2W = ["x2w0", "x2w1"]
        base_x2w2 = [CNT[nm] for nm in X2W]
        @block.gpsimd
        def _(g):
            W(g, "spl", base_spl + 16 * NDB)
            for hb in range(NDB):
                if hb >= 2:
                    W(g, "vd", base_vd + hb - 1)
                ii = g.dma_start(out=xst[:, hb % 2, 0:WTOK], in_=xsp[w, hb, :, :])
                inc(ii, XAR[hb % 2], 16)
                if hb >= 1:
                    W(g, "vd", base_vd + hb)
                    ii = g.dma_start(out=x2s[w, hb - 1, :, :], in_=x2t[:, (hb - 1) % 2, :])
                    inc(ii, X2W[(hb - 1) % 2], 16)
            W(g, "vd", base_vd + NDB)
            ii = g.dma_start(out=x2s[w, NDB - 1, :, :], in_=x2t[:, (NDB - 1) % 2, :])
            inc(ii, X2W[(NDB - 1) % 2], 16)
        @block.tensor
        def _(t):
            W(t, "smt")
            for hb in range(NDB):
                for hh in range(2):
                    idx = hb * 2 + hh
                    if hb % 2 == 0:
                        W(t, "dy", base_dy + 16 * (2 * (hb // 2) + 2))
                    else:
                        W(t, "ds_", base_ds + 16 * (2 * (hb // 2) + 2))
                    if hb >= 2 and hh == 0:
                        W(t, "vd", base_vd + hb - 1)
                    ii = None
                    for k in range(NIBH):
                        j = hh * NIBH + k
                        ii = t.matmul(PB[hb % 2][:, 0:WTOK], wsd[:, idx % 3, k, :],
                                      pool[:, :, j, :],
                                      start=(j == 0), stop=(j == NIB - 1))
                    inc(ii, "td", 1)
        @block.vector
        def _(v):
            W(v, "outa")
            W(v, "outb")
            for hb in range(NDB):
                W(v, "td", base_td + 2 * (hb + 1))
                W(v, XAR[hb % 2], base_xar3[hb % 2] + 16 * (hb // 2 + 1))
                if hb >= 2:
                    W(v, X2W[hb % 2], base_x2w2[hb % 2] + 16 * (hb // 2))
                    W(v, "sx3", base_sx3 + hb - 1)
                ii = v.tensor_tensor(x2t[:, hb % 2, :], PB[hb % 2][:, 0:WTOK],
                                     xst[:, hb % 2, 0:WTOK], ALU.add)
                inc(ii, "vd", 1)
        base_sx3b = base_sx3
        base_tst3 = CNT["tst"]
        @block.vector
        def _(v):
            for db in range(NDB):
                W(v, "sx3", base_sx3b + db + 1)
                if db >= 2:
                    W(v, "tst", base_tst3 + db - 1)
                ii = v.tensor_tensor(sqst[:, db % 2, :], xn[:, db, :], xn[:, db, :], ALU.mult)
                inc(ii, "vsq", 1)
        base_vsq3 = CNT["vsq"] - NDB
        @block.tensor
        def _(t):
            for db in range(NDB):
                W(t, "vsq", base_vsq3 + db + 1)
                t.matmul(PB[2][0:1, 0:WTOK], onec[:, :], xn[:, db, :],
                         start=(db == 0), stop=(db == NDB - 1))
                ii = t.matmul(PB[3][0:1, 0:WTOK], onec[:, :], sqst[:, db % 2, :],
                              start=(db == 0), stop=(db == NDB - 1))
                inc(ii, "tst", 1)
        base_x2r = CNT["x2r"]
        @block.gpsimd
        def _(g):
            W(g, "td")
            W(g, "x2w0")
            W(g, "x2w1")
            for db in range(NDB):
                ii = g.dma_start(out=xT[:, db, :], in_=x2s[w, db, :, :])
                inc(ii, "x2r", 16)
        if w == 0 and "x2" in taps:
            @block.sync
            def _(s):
                W(s, "x2r")
                ii = s.dma_start(out=taps["x2"][:, :, :], in_=xT[:, :, :])
                inc(ii, "out", 16)
        @block.vector
        def _(v):
            W(v, "x2r", base_x2r + 16 * NDB)
            v.engine_nop()
        ln_finalize_normalize()
        ffn(w, second=True)

    @block.gpsimd
    def _(g):
        W(g, "outa")
        W(g, "outb")
        W(g, "out")

    es.close()
    ic = {k: v for k, v in CNT.items()}
    print(f"[build] done; sem totals sample: out={ic['out']}, tg={ic['tg']}, td={ic['td']}")
    return nc, taps


_NC_CACHE = {}

def get_nc(key="main", cfg=None, debug=()):
    if key not in _NC_CACHE:
        _NC_CACHE[key] = build_nc(cfg or CFG, debug)
    return _NC_CACHE[key]


def make_inmaps(cfg, x, wg, wu, wd, wq, sc, bi):
    NC = cfg["NCORES"]
    TOK = cfg["WTOK"] * cfg["WAVES"]
    D = cfg["D"]
    x = np.ascontiguousarray(np.asarray(x, dtype=np.float32).reshape(NC * TOK, D))
    sc2 = np.ascontiguousarray(np.asarray(sc, dtype=np.float32).reshape(D // 128, 128).T)
    bi2 = np.ascontiguousarray(np.asarray(bi, dtype=np.float32).reshape(D // 128, 128).T)
    ones = np.ones((1, 128), dtype=bfnp)
    onec_ = np.ones((128, 1), dtype=bfnp)
    idb_ = np.eye(128, dtype=bfnp)
    idf_ = np.eye(128, dtype=np.float32)
    wg = np.ascontiguousarray(np.asarray(wg, dtype=np.float32))
    wu = np.ascontiguousarray(np.asarray(wu, dtype=np.float32))
    wd = np.ascontiguousarray(np.asarray(wd, dtype=np.float32))
    wq = np.ascontiguousarray(np.asarray(wq, dtype=np.int32))
    in_maps = []
    for c in range(NC):
        in_maps.append({
            "x_in": x[c * TOK:(c + 1) * TOK],
            "wg_f": wg, "wu_f": wu, "wd_f": wd, "wq_i": wq,
            "sc_in": sc2, "bi_in": bi2,
            "ones_in": ones, "onec_in": onec_, "idb_in": idb_, "idf_in": idf_,
        })
    return in_maps


def kernel(L0_attn_x, L0_attn_w_gate, L0_attn_w_up, L0_attn_w_down,
           L0_ffn_weight_int8, L0_ffn_scale, L0_ffn_bias):
    cfg = CFG
    NC = cfg["NCORES"]
    TOK = cfg["WTOK"] * cfg["WAVES"]
    D = cfg["D"]
    nc, _ = get_nc("main", cfg, ())
    in_maps = make_inmaps(cfg, L0_attn_x, L0_attn_w_gate, L0_attn_w_up,
                          L0_attn_w_down, L0_ffn_weight_int8, L0_ffn_scale, L0_ffn_bias)
    res = run_bass_kernel_spmd(nc, in_maps, core_ids=list(range(NC)))
    out = np.concatenate([np.asarray(r["out_t"]) for r in res.results], axis=0)
    return out.reshape(1, NC * TOK, D).astype(np.float32)
